# version 07 (final)
# baseline (speedup 1.0000x reference)
import os
import contextlib
import numpy as np
import ml_dtypes
import concourse.bass as bass
import concourse.mybir as mybir
from concourse.bass_utils import run_bass_kernel_spmd

F32 = mybir.dt.float32
BF16 = mybir.dt.bfloat16
AF = mybir.ActivationFunctionType
ALU = mybir.AluOpType
AX = mybir.AxisListType
BF = ml_dtypes.bfloat16

T = 2048
D = 1024
NCH = 8
NTT = 16
H = 16
HD = 64
DEPTH = 6
DI = 5460
DIH = 2730
NJ = 22
CST = 2052
WIN = 256

N_BATCH = int(os.environ.get("KK_NBATCH", "2"))
N_LAYERS = int(os.environ.get("KK_NLAYERS", str(DEPTH)))
PHASES = os.environ.get("KK_PHASES", "all")


def _f32(x):
    return np.ascontiguousarray(x, dtype=np.float32)


def host_constants():
    c = {}
    c["ident_f"] = np.eye(128, dtype=np.float32)
    c["ident_b"] = np.eye(128, dtype=BF)
    k = np.arange(128)[:, None]
    q = np.arange(128)[None, :]
    c["maskA"] = (k <= q).astype(BF)
    c["maskB"] = (k >= q).astype(BF)
    half = np.arange(0, HD, 2, dtype=np.float64)
    inv_freq = 1.0 / 10000.0 ** (half / HD)
    t = np.arange(T, dtype=np.float64)
    freqs = t[:, None] * inv_freq[None, :]
    freqs = np.concatenate([freqs, freqs], -1)
    base_scale = (half + 0.4 * HD) / (1.4 * HD)
    power = (t - T // 2) / (WIN // 2)
    scale = base_scale[None, :] ** power[:, None]
    scale = np.concatenate([scale, scale], -1)
    sgn = np.where(np.arange(HD) < 32, -1.0, 1.0)[None, :]
    tabs = {
        "rcq": np.cos(freqs) * scale,
        "rsq": np.sin(freqs) * scale * sgn,
        "rck": np.cos(freqs) / scale,
        "rsk": np.sin(freqs) / scale * sgn,
    }
    for name, tab in tabs.items():
        c[name] = (tab.reshape(NTT, 128, HD).transpose(1, 0, 2)
                   .reshape(128, NTT * HD).astype(BF))
    c["ones_row"] = np.ones((1, 128), dtype=BF)
    c["ones_col"] = np.ones((128, 1), dtype=BF)
    return c


def host_layouts(inp):
    d = {}
    d["wq_f"] = _f32(np.transpose(inp["L0_attn_Wq"], (1, 0, 2)).reshape(D, H * HD))
    d["wk_f"] = _f32(np.transpose(inp["L0_attn_Wk"], (1, 0, 2)).reshape(D, H * HD))
    d["wvd_f"] = _f32(np.transpose(inp["L0_attn_Wvd"], (1, 0, 2)).reshape(D, H * 16))
    d["wvu_f"] = _f32(np.transpose(inp["L0_attn_Wvu"], (1, 0, 2)).reshape(16, H * HD))
    d["bq_t"] = _f32(np.asarray(inp["L0_attn_bq"]).T)
    d["bk_t"] = _f32(np.asarray(inp["L0_attn_bk"]).T)
    d["bvd_r"] = _f32(np.asarray(inp["L0_attn_bvd"]).reshape(1, H * 16))
    d["bvu_r"] = _f32(np.asarray(inp["L0_attn_bvu"]).reshape(1, H * HD))
    qs = _f32(inp["L0_ffn_q_scale"])
    ks = _f32(inp["L0_ffn_k_scale"])
    d["qs_r"], d["ks_r"] = _f32(qs.reshape(1, -1)), _f32(ks.reshape(1, -1))
    g = _f32(inp["L0_ffn_ln_g"])
    bb = _f32(inp["L0_ffn_ln_b"])
    d["g_cols"] = g.reshape(DEPTH, NCH, 128).transpose(2, 0, 1).copy()
    d["b_cols"] = bb.reshape(DEPTH, NCH, 128).transpose(2, 0, 1).copy()
    d["b2_cols"] = (_f32(inp["L0_ffn_b2"]).reshape(DEPTH, NCH, 128)
                    .transpose(2, 0, 1).copy())
    b1 = _f32(inp["L0_ffn_b1"])
    b1p = np.zeros((DEPTH, 2, NJ * 128), dtype=np.float32)
    b1p[:, 0, :DIH] = b1[:, :DIH]
    b1p[:, 1, :DIH] = b1[:, DIH:]
    d["b1_cols"] = b1p.reshape(DEPTH, 2, NJ, 128).transpose(3, 0, 1, 2).copy()
    d["wqkv"] = _f32(inp["L0_ffn_Wqkv"])
    d["wo"] = _f32(inp["L0_ffn_Wo"])
    d["w1"] = _f32(inp["L0_ffn_W1"])
    d["w2"] = _f32(inp["L0_ffn_W2"])
    return d


CONST_SHAPES = {
    "ident_f": ([128, 128], F32), "ident_b": ([128, 128], BF16),
    "maskA": ([128, 128], BF16), "maskB": ([128, 128], BF16),
    "rcq": ([128, NTT * HD], BF16), "rsq": ([128, NTT * HD], BF16),
    "rck": ([128, NTT * HD], BF16), "rsk": ([128, NTT * HD], BF16),
    "ones_row": ([1, 128], BF16), "ones_col": ([128, 1], BF16),
}
WEIGHT_SHAPES = {
    "wq_f": [D, H * HD], "wk_f": [D, H * HD], "wvd_f": [D, H * 16],
    "wvu_f": [16, H * HD], "bq_t": [HD, H], "bk_t": [HD, H],
    "bvd_r": [1, H * 16], "bvu_r": [1, H * HD],
    "qs_r": [1, DEPTH * HD], "ks_r": [1, DEPTH * HD],
    "g_cols": [128, DEPTH, NCH], "b_cols": [128, DEPTH, NCH],
    "b2_cols": [128, DEPTH, NCH], "b1_cols": [128, DEPTH, 2, NJ],
    "wqkv": [DEPTH, D, 3 * D], "wo": [DEPTH, D, D],
    "w1": [DEPTH, D, DI], "w2": [DEPTH, DIH, D],
}
SMALLS = ["bq_t", "bk_t", "bvd_r", "bvu_r", "qs_r", "ks_r",
          "g_cols", "b_cols", "b2_cols", "b1_cols"]


class KB:

    def __init__(self, n_batch, n_layers, phases):
        self.n_batch = n_batch
        self.n_layers = n_layers
        self.phases = phases
        self.nc = bass.Bass("TRN2", target_bir_lowering=False)
        self.dbg_outs = []
        self.dump_names = set(
            x for x in os.environ.get("KK_DUMPS", "").split(",") if x)

    def inc(self, instr, sem, k=1):
        instr.then_inc(self.sems[sem], k)
        self.cnt[sem] += k
        return self.cnt[sem]

    def wait(self, eng, engname, sem, val=None):
        if val is None:
            val = self.cnt[sem]
        if val <= 0:
            return
        key = (engname, sem)
        if self.wm.get(key, -1) >= val:
            return
        eng.wait_ge(self.sems[sem], val)
        self.wm[key] = val

    def v(self, off, shape, dtype=BF16):
        n = shape[1] * (2 if dtype == F32 else 1)
        if os.environ.get("KK_VLOG"):
            import traceback
            fr = traceback.extract_stack()[-2]
            self._vlog = getattr(self, "_vlog", [])
            self._vlog.append((off, off + n, shape, str(dtype), fr.lineno))
        ap = self.AR[:shape[0], off:off + n]
        if dtype != BF16:
            ap = ap.bitcast(dtype)
        return ap

    def self_wait(self, eng, sem):
        eng.wait_ge(self.sems[sem], self.cnt[sem])
        self.wm[(sem, sem)] = self.cnt[sem]

    def maybe_dump(self, name, sbuf_ap):
        if name in self.dump_names:
            self.dbg_dump(name, sbuf_ap)

    def dbg_dump(self, name, sbuf_ap, engname="s"):
        shp = list(sbuf_ap.shape)
        t = self.nc.dram_tensor(name, shp, sbuf_ap.dtype, kind="ExternalOutput").ap()
        self.dbg_outs.append(name)
        blocksec = getattr(self.block, "gpsimd")

        @blocksec
        def _(g):
            for s in ["t", "v", "s", "g"]:
                self.wait(g, "g", s)
            i = g.dma_start(out=t, in_=sbuf_ap)
            self.inc(i, "dg", 16)

    def build(self):
        nc = self.nc
        self.X_in = nc.dram_tensor("x_in", [2, T, D], F32, kind="ExternalInput").ap()
        self.OUT = nc.dram_tensor("out", [2, T, D], F32, kind="ExternalOutput").ap()
        self.YSP = nc.dram_tensor("y_spill", [2, D, T], F32, kind="Internal").ap()
        self.C = {n: nc.dram_tensor(n, shp, dt, kind="ExternalInput").ap()
                  for n, (shp, dt) in CONST_SHAPES.items()}
        self.W = {n: nc.dram_tensor(n, list(shp), F32, kind="ExternalInput").ap()
                  for n, shp in WEIGHT_SHAPES.items()}

        with contextlib.ExitStack() as st:
            E = st.enter_context
            nc_ = self.nc
            self.X = E(nc_.sbuf_tensor("X", [128, NCH * CST], F32))
            self.idf = E(nc_.sbuf_tensor("idf", [128, 128], F32))
            self.idb = E(nc_.sbuf_tensor("idb", [128, 128], BF16))
            self.mskA = E(nc_.sbuf_tensor("mskA", [128, 128], BF16))
            self.mskB = E(nc_.sbuf_tensor("mskB", [128, 128], BF16))
            self.rtab = {n: E(nc_.sbuf_tensor("sb_" + n, [128, NTT * HD], BF16))
                         for n in ["rcq", "rsq", "rck", "rsk"]}
            self.onesr = E(nc_.sbuf_tensor("onesr", [1, 128], BF16))
            self.onesc = E(nc_.sbuf_tensor("onesc", [128, 1], BF16))
            self.sm = {n: E(nc_.sbuf_tensor("sb_" + n, list(WEIGHT_SHAPES[n]), F32))
                       for n in SMALLS}
            self.AR = E(nc_.sbuf_tensor("AR", [128, 59392], BF16))
            self.STG = E(nc_.sbuf_tensor("STG", [128, 1536], F32))
            self.ps = [E(nc_.psum_tensor(f"ps{i}", [128, 512], F32))
                       for i in range(7)]
            self.psb = E(nc_.psum_tensor("psb", [128, 1024], BF16))

            self.block = E(nc_.Block())
            self.sems = {s: E(nc_.semaphore(s))
                         for s in ["t", "v", "s", "g", "dw", "dg"]}
            self.cnt = {s: 0 for s in self.sems}
            self.wm = {}
            self._stage_gate = {}

            self.emit()
        return self.nc

    def emit(self):
        self.emit_setup()
        for b in range(self.n_batch):
            self.emit_load_x(b)
            if self.phases != "p0":
                self.emit_outer(b)
            if self.phases in ("ln1", "inner", "all", "fin"):
                self.emit_postln(b, spill=(self.phases != "ln1"))
            if self.phases in ("inner", "all"):
                for l in range(self.n_layers):
                    self.emit_inner_attn(b, l)
                    if self.phases == "all" or os.environ.get("KK_FFN"):
                        self.emit_inner_ffn(b, l)
            if self.phases in ("fin", "all"):
                self.emit_final(b)
            self.emit_store_out(b)
        self.emit_tail()

    def emit_setup(self):
        block = self.block
        C, W, sm = self.C, self.W, self.sm

        @block.sync
        def _(sync):
            for n in ["ident_f", "ident_b", "maskA", "maskB", "ones_row",
                      "ones_col"]:
                dst = {"ident_f": self.idf, "ident_b": self.idb,
                       "maskA": self.mskA, "maskB": self.mskB,
                       "ones_row": self.onesr, "ones_col": self.onesc}[n]
                self.inc(sync.dma_start(out=dst[:], in_=C[n]), "dw", 16)
            for n in ["rcq", "rsq", "rck", "rsk"]:
                self.inc(sync.dma_start(out=self.rtab[n][:], in_=C[n]), "dw", 16)
            for n in SMALLS:
                self.inc(sync.dma_start(out=sm[n][:], in_=W[n]), "dw", 16)
        self.setup_dw = self.cnt["dw"]

        @block.gpsimd
        def _(g):
            for c in range(NCH):
                i = g.memset(self.X[:, c * CST:c * CST + 1], 0.0)
            self.inc(i, "g")

    def emit_load_x(self, b):
        nc, block = self.nc, self.block
        X, STG, idf, ps = self.X, self.STG, self.idf, self.ps
        for tt in range(NTT):
            @block.sync
            def _(sync):
                self.wait(sync, "sy", "t")
                self.wait(sync, "sy", "dw")
                i = sync.dma_start(out=STG[:, :1024],
                                   in_=self.X_in[b, tt * 128:(tt + 1) * 128, :])
                self.inc(i, "dw", 16)
            dmac = self.cnt["dw"]

            for half in range(2):
                @block.tensor
                def _(t):
                    self.wait(t, "t", "dw", dmac)
                    self.wait(t, "t", "s")
                    for cc in range(4):
                        c = half * 4 + cc
                        i = nc.tensor.transpose(
                            out=ps[cc][:, :128],
                            in_=STG[:, c * 128:(c + 1) * 128],
                            identity=idf[:])
                    self.inc(i, "t")
                tq = self.cnt["t"]

                @block.scalar
                def _(s):
                    self.wait(s, "s", "t", tq)
                    for cc in range(4):
                        c = half * 4 + cc
                        i = nc.scalar.copy(
                            out=X[:, c * CST + 1 + tt * 128:
                                  c * CST + 1 + (tt + 1) * 128],
                            in_=ps[cc][:, :128])
                    self.inc(i, "s")

    def emit_store_out(self, b):
        nc, block = self.nc, self.block
        X, STG, idf, ps = self.X, self.STG, self.idf, self.ps
        for tt in range(NTT):
            for half in range(2):
                @block.tensor
                def _(t):
                    self.wait(t, "t", "s")
                    self.wait(t, "t", "dw")
                    self.wait(t, "t", "v")
                    for cc in range(4):
                        c = half * 4 + cc
                        i = nc.tensor.transpose(
                            out=ps[cc][:, :128],
                            in_=X[:, c * CST + 1 + tt * 128:
                                  c * CST + 1 + (tt + 1) * 128],
                            identity=idf[:])
                    self.inc(i, "t")
                tq = self.cnt["t"]

                @block.scalar
                def _(s):
                    self.wait(s, "s", "t", tq)
                    for cc in range(4):
                        c = half * 4 + cc
                        i = nc.scalar.copy(out=STG[:, c * 128:(c + 1) * 128],
                                           in_=ps[cc][:, :128])
                    self.inc(i, "s")
            sq = self.cnt["s"]

            @block.sync
            def _(sync):
                self.wait(sync, "sy", "s", sq)
                i = sync.dma_start(out=self.OUT[b, tt * 128:(tt + 1) * 128, :],
                                   in_=STG[:, :1024])
                self.inc(i, "dw", 16)


    def load_cast(self, dram_ap, dst_bf, stage_f32, pre_gate=()):
        nc, block = self.nc, self.block
        prev_cast = self._stage_gate.get(stage_f32.offset, 0)

        @block.sync
        def _(sync):
            self.wait(sync, "sy", "v", prev_cast)
            for sem, val in pre_gate:
                self.wait(sync, "sy", sem, val)
            i = sync.dma_start(out=stage_f32, in_=dram_ap)
            self.inc(i, "dw", 16)
        dmac = self.cnt["dw"]

        @block.vector
        def _(v):
            self.wait(v, "v", "dw", dmac)
            self.wait(v, "v", "t")
            i = v.tensor_copy(out=dst_bf, in_=stage_f32)
            self.inc(i, "v")
        self._stage_gate[stage_f32.offset] = self.cnt["v"]

    def bcast_rows(self, rows_f32, tmp_hi, tmp_lo_f, tmp_lo_b, psum_out, n):
        nc, block = self.nc, self.block

        @block.vector
        def _(v):
            self.wait(v, "v", "t")
            i = v.tensor_copy(out=tmp_hi, in_=rows_f32)
            self.inc(i, "v")
            self.self_wait(v, "v")
            i = v.tensor_tensor(out=tmp_lo_f, in0=rows_f32, in1=tmp_hi,
                                op=ALU.subtract)
            self.inc(i, "v")
            self.self_wait(v, "v")
            i = v.tensor_copy(out=tmp_lo_b, in_=tmp_lo_f)
            self.inc(i, "v")
        vq = self.cnt["v"]

        @block.tensor
        def _(t):
            self.wait(t, "t", "v", vq)
            self.wait(t, "t", "s")
            nc.tensor.matmul(out=psum_out, lhsT=self.onesr[:], rhs=tmp_hi,
                             start=True, stop=False)
            i = nc.tensor.matmul(out=psum_out, lhsT=self.onesr[:], rhs=tmp_lo_b,
                                 start=False, stop=True)
            self.inc(i, "t")

    def m_row_to_dst(self, mc_f32, mc_bf, mrst_f, mrsb, psum, dst_row_ap, nqb):
        nc, block = self.nc, self.block

        @block.vector
        def _(v):
            self.self_wait(v, "v")
            self.wait(v, "v", "t")
            i = v.tensor_copy(out=mc_bf, in_=mc_f32)
            self.inc(i, "v")
        vq = self.cnt["v"]

        @block.tensor
        def _(t):
            self.wait(t, "t", "v", vq)
            self.wait(t, "t", "s")
            i = nc.tensor.matmul(out=psum[:nqb, :128], lhsT=mc_bf, rhs=self.idb[:],
                                 start=True, stop=True)
            self.inc(i, "t")
        tq = self.cnt["t"]

        @block.scalar
        def _(sc):
            self.wait(sc, "s", "t", tq)
            self.wait(sc, "s", "v")
            i = nc.scalar.copy(out=mrst_f[:nqb, :], in_=psum[:nqb, :128])
            self.inc(i, "s")
        sq = self.cnt["s"]

        @block.vector
        def _(v):
            self.wait(v, "v", "s", sq)
            self.wait(v, "v", "dg")
            i = v.tensor_copy(out=mrsb[:nqb, :], in_=mrst_f[:nqb, :])
            self.inc(i, "v")
        vq2 = self.cnt["v"]

        @block.gpsimd
        def _(g):
            self.wait(g, "g", "v", vq2)
            i = g.dma_start(out=dst_row_ap, in_=mrsb[:nqb, :])
            self.inc(i, "dg", 16)

    def emit_outer(self, b):
        nc, block = self.nc, self.block
        X, W, sm = self.X, self.W, self.sm
        ps, psb = self.ps, self.psb
        v = self.v
        XB = [v(0 + c * 2048, (128, 2048)) for c in range(NCH)]
        QQ = [v(16384 + i * 2048, (65, 2048)) for i in range(2)]
        KK = [v(20480 + i * 2048, (65, 2048)) for i in range(2)]
        VDtok = v(24576, (128, 4096))
        VDT = [v(28672 + i * 2048, (16, 2048)) for i in range(2)]
        AOP = v(52736, (128, 2048), F32)
        VEXT = v(32768, (128, 1056))
        ATT = [v(34944 + i * 1024, (128, 1024)) for i in range(2)]
        bvd_bc = v(36992, (128, 256), F32)
        bvu_bc = v(37504, (128, 64), F32)
        bqcol = v(37632, (128, 1), F32)
        bkcol = v(37636, (128, 1), F32)
        mc_f = v(37664, (128, 16), F32)
        mrst_f = v(37696, (16, 128), F32)
        rl_f = v(37952, (128, 4), F32)
        tmax4 = v(37960, (128, 4), F32)
        mc_b = v(37968, (128, 16))
        mrsb = v(37984, (16, 128))
        brow_b = v(38112, (1, 256))
        WQ4 = [v(38400, (128, 4096))] * 2
        WK4 = [v(42496, (128, 4096))] * 2
        WVD = v(56832, (128, 2048))
        WVU = v(51200, (16, 1024))
        STAGE = [v(46592 + i * 2048, (128, 1024), F32) for i in range(2)]
        STAGEB = v(50688, (128, 256), F32)

        @block.vector
        def _(ve):
            self.wait(ve, "v", "s")
            for c in range(NCH):
                i = ve.tensor_copy(out=XB[c],
                                   in_=X[:, c * CST + 1:c * CST + 1 + 2048])
            self.inc(i, "v")
        xb_ready = self.cnt["v"]

        for c in range(NCH):
            self.load_cast(
                W["wvd_f"][c * 128:(c + 1) * 128, :],
                WVD[:, c * 256:(c + 1) * 256],
                STAGEB[:, :256])
        wvd_ready = self.cnt["v"]
        @block.vector
        def _(ve):
            self.wait(ve, "v", "dw")
            i = ve.tensor_copy(out=brow_b[:, :256], in_=sm["bvd_r"][:, :])
            self.inc(i, "v")
        vq = self.cnt["v"]

        @block.tensor
        def _(t):
            self.wait(t, "t", "v", vq)
            i = nc.tensor.matmul(out=ps[6][:, :256], lhsT=self.onesr[:],
                                 rhs=brow_b[:, :256], start=True, stop=True)
            self.inc(i, "t")
        tq = self.cnt["t"]

        @block.scalar
        def _(sc):
            self.wait(sc, "s", "t", tq)
            i = nc.scalar.copy(out=bvd_bc, in_=ps[6][:, :256])
            self.inc(i, "s")
        bvd_ready = self.cnt["s"]

        for tt in range(NTT):
            @block.tensor
            def _(t):
                self.wait(t, "t", "v", max(xb_ready, wvd_ready))
                if tt >= 2:
                    self.wait(t, "t", "v", self._o2gate[tt - 2])
                bank = ps[tt % 2]
                for c in range(NCH):
                    i = nc.tensor.matmul(
                        out=bank[:, :256],
                        lhsT=XB[c][:, tt * 128:(tt + 1) * 128],
                        rhs=WVD[:, c * 256:(c + 1) * 256],
                        start=(c == 0), stop=(c == NCH - 1))
                self.inc(i, "t")
            tq = self.cnt["t"]

            @block.vector
            def _(ve):
                self.wait(ve, "v", "t", tq)
                self.wait(ve, "v", "s", bvd_ready)
                i = ve.tensor_tensor(out=VDtok[:, tt * 256:(tt + 1) * 256],
                                     in0=ps[tt % 2][:, :256], in1=bvd_bc,
                                     op=ALU.add)
                self.inc(i, "v")
            if not hasattr(self, "_o2gate"):
                self._o2gate = {}
            self._o2gate[tt] = self.cnt["v"]

        self.load_cast(W["wvu_f"], WVU, v(46592, (16, 1024), F32))
        wvu_ready = self.cnt["v"]
        self.maybe_dump("wvub", WVU)

        for p in range(8):
            hA, hB = 2 * p, 2 * p + 1
            grp, sl = p // 4, p % 4
            if sl == 0:
                gate = (("t", self.cnt["t"]), ("dg", self.cnt["dg"]))
                for c in range(NCH):
                    self.load_cast(
                        W["wq_f"][c * 128:(c + 1) * 128,
                                  grp * 512:(grp + 1) * 512],
                        WQ4[grp % 2][:, c * 512:(c + 1) * 512],
                        STAGE[0][:, :512], pre_gate=gate)
                    self.load_cast(
                        W["wk_f"][c * 128:(c + 1) * 128,
                                  grp * 512:(grp + 1) * 512],
                        WK4[grp % 2][:, c * 512:(c + 1) * 512],
                        STAGE[1][:, :512], pre_gate=gate)
            w_ready = self.cnt["v"]

            @block.scalar
            def _(sc):
                self.wait(sc, "s", "dw", self.setup_dw)
                self.wait(sc, "s", "t")
                nc.scalar.copy(out=bqcol[0:64, :], in_=sm["bq_t"][:, hA:hA + 1])
                nc.scalar.copy(out=bqcol[64:128, :], in_=sm["bq_t"][:, hB:hB + 1])
                nc.scalar.copy(out=bkcol[0:64, :], in_=sm["bk_t"][:, hA:hA + 1])
                i = nc.scalar.copy(out=bkcol[64:128, :], in_=sm["bk_t"][:, hB:hB + 1])
                self.inc(i, "s")
            bias_ready = self.cnt["s"]

            @block.vector
            def _(ve):
                self.wait(ve, "v", "t")
                ve.memset(KK[0][64:65, :], 1.0)
                i = ve.memset(KK[1][64:65, :], 1.0)
                self.inc(i, "v")

            for tc in range(4):
                for which, WW, DD, bcol in ((0, WQ4[grp % 2], QQ, bqcol),
                                            (1, WK4[grp % 2], KK, bkcol)):
                    @block.tensor
                    def _(t):
                        self.wait(t, "t", "v", max(w_ready, xb_ready))
                        self.wait(t, "t", "s")
                        bank = ps[2 + which]
                        for c in range(NCH):
                            i = nc.tensor.matmul(
                                out=bank[:, :512],
                                lhsT=WW[:, c * 512 + sl * 128:
                                        c * 512 + sl * 128 + 128],
                                rhs=XB[c][:, tc * 512:(tc + 1) * 512],
                                start=(c == 0), stop=(c == NCH - 1))
                        self.inc(i, "t")
                    tq = self.cnt["t"]

                    @block.scalar
                    def _(sc):
                        self.wait(sc, "s", "t", tq)
                        bank = ps[2 + which]
                        nc.scalar.activation(
                            out=DD[0][0:64, tc * 512:(tc + 1) * 512],
                            in_=bank[0:64, :512], func=AF.Identity,
                            bias=bcol[0:64, :], scale=1.0)
                        i = nc.scalar.activation(
                            out=DD[1][0:64, tc * 512:(tc + 1) * 512],
                            in_=bank[64:128, :512], func=AF.Identity,
                            bias=bcol[64:128, :], scale=1.0)
                        self.inc(i, "s")
            proj_done_s = self.cnt["s"]
            if p == 0:
                self.maybe_dump("qq0", QQ[0][:, :])
                self.maybe_dump("kk0", KK[0][:, :])
                self.maybe_dump("vdtok", VDtok)

            for hi, h in enumerate((hA, hB)):
                self.emit_outer_head(
                    b, h, hi, QQ[hi], KK[hi], VDtok, VDT[hi], AOP, VEXT,
                    ATT, mc_f, mc_b, mrst_f, mrsb, rl_f, tmax4,
                    bvu_bc, brow_b, WVU, proj_done_s)

            if p == 0:
                self.maybe_dump("aop0", AOP)
            for grp in range(4):
                @block.tensor
                def _(t):
                    self.wait(t, "t", "s")
                    self.wait(t, "t", "v")
                    for j in range(4):
                        tt = grp * 4 + j
                        i = nc.tensor.transpose(
                            out=ps[4][:, j * 128:(j + 1) * 128],
                            in_=AOP[:, tt * 128:(tt + 1) * 128],
                            identity=self.idf[:])
                    self.inc(i, "t")
                tq = self.cnt["t"]

                @block.vector
                def _(ve):
                    self.wait(ve, "v", "t", tq)
                    base = self.X[:, p * CST + 1 + grp * 512:
                                  p * CST + 1 + (grp + 1) * 512]
                    i = ve.tensor_tensor(out=base, in0=base,
                                         in1=ps[4][:, :512], op=ALU.add)
                    self.inc(i, "v")

    def emit_outer_head(self, b, h, hi, QQh, KKh, VDtok, VDTh, AOP, VEXT, ATT,
                        mc_f, mc_b, mrst_f, mrsb, rl_f, tmax4,
                        bvu_bc, brow_b, WVU, proj_done_s):
        nc, block = self.nc, self.block
        ps, psb = self.ps, self.psb

        for grp in range(2):
            @block.tensor
            def _(t):
                self.wait(t, "t", "v", self._o2gate[NTT - 1])
                self.wait(t, "t", "s")
                for j in range(8):
                    tt = grp * 8 + j
                    i = nc.tensor.transpose(
                        out=psb[0:16, j * 128:(j + 1) * 128],
                        in_=VDtok[:, tt * 256 + h * 16: tt * 256 + (h + 1) * 16],
                        identity=self.idb[:])
                self.inc(i, "t")
            tq = self.cnt["t"]

            @block.scalar
            def _(sc):
                self.wait(sc, "s", "t", tq)
                i = nc.scalar.copy(out=VDTh[0:16, grp * 1024:(grp + 1) * 1024],
                                   in_=psb[0:16, :1024])
                self.inc(i, "s")
        vdt_done = self.cnt["s"]

        if h == 1:
            self.maybe_dump("vdt1", VDTh)
        @block.vector
        def _(ve):
            self.wait(ve, "v", "dw", self.setup_dw)
            self.wait(ve, "v", "s")
            self.wait(ve, "v", "t")
            i = ve.tensor_copy(out=brow_b[:, :64],
                               in_=self.sm["bvu_r"][:, h * 64:(h + 1) * 64])
            self.inc(i, "v")
        vq = self.cnt["v"]

        @block.tensor
        def _(t):
            self.wait(t, "t", "v", vq)
            i = nc.tensor.matmul(out=ps[6][:, :64], lhsT=self.onesr[:],
                                 rhs=brow_b[:, :64], start=True, stop=True)
            self.inc(i, "t")
        tq = self.cnt["t"]

        @block.scalar
        def _(sc):
            self.wait(sc, "s", "t", tq)
            self.wait(sc, "s", "v")
            i = nc.scalar.copy(out=bvu_bc, in_=ps[6][:, :64])
            self.inc(i, "s")
        bvu_ready = self.cnt["s"]
        if h == 1:
            self.maybe_dump("bvubc1", bvu_bc)

        @block.vector
        def _(ve):
            self.wait(ve, "v", "t")
            i = ve.memset(VEXT.rearrange("p (t u) -> p t u", u=66)[:, :, 64:66], 0.0)
            i = ve.memset(VEXT.rearrange("p (t u) -> p t u", u=66)[:, :, 64:65], 1.0)
            self.inc(i, "v")
        ones_done = self.cnt["v"]

        for grp in range(2):
            @block.tensor
            def _(t):
                self.wait(t, "t", "s", max(vdt_done, bvu_ready))
                self.wait(t, "t", "v")
                for j in range(8):
                    tt = grp * 8 + j
                    i = nc.tensor.matmul(
                        out=ps[2 + grp][:, j * 64:(j + 1) * 64],
                        lhsT=VDTh[0:16, tt * 128:(tt + 1) * 128],
                        rhs=WVU[0:16, h * 64:(h + 1) * 64],
                        start=True, stop=True)
                self.inc(i, "t")
            tq = self.cnt["t"]

            @block.vector
            def _(ve):
                self.wait(ve, "v", "t", tq)
                for j in range(8):
                    tt = grp * 8 + j
                    i = ve.tensor_tensor(
                        out=VEXT.rearrange("p (t u) -> p t u", u=66)[:, tt, 0:64],
                        in0=ps[2 + grp][:, j * 64:(j + 1) * 64],
                        in1=bvu_bc, op=ALU.add)
                self.inc(i, "v")
        vext_done = self.cnt["v"]

        for qb in range(NTT):
            nk = T - qb * 128
            nchk = (nk + 511) // 512
            for ci in range(nchk):
                w = min(512, nk - ci * 512)

                @block.tensor
                def _(t):
                    self.wait(t, "t", "s", proj_done_s)
                    self.wait(t, "t", "v")
                    i = nc.tensor.matmul(
                        out=ps[ci % 2][:, :w],
                        lhsT=QQh[0:64, qb * 128:(qb + 1) * 128],
                        rhs=KKh[0:64, qb * 128 + ci * 512: qb * 128 + ci * 512 + w],
                        start=True, stop=True)
                    self.inc(i, "t")
                tq = self.cnt["t"]

                @block.vector
                def _(ve):
                    self.wait(ve, "v", "t", tq)
                    i = ve.tensor_reduce(out=tmax4[:, ci:ci + 1],
                                         in_=ps[ci % 2][:, :w],
                                         axis=AX.X, op=ALU.max)
                    self.inc(i, "v")

            @block.vector
            def _(ve):
                self.self_wait(ve, "v")
                i = ve.tensor_reduce(out=mc_f[:, qb:qb + 1],
                                     in_=tmax4[:, :nchk], axis=AX.X,
                                     op=ALU.max, negate=True)
                self.inc(i, "v")

        self.m_row_to_dst(mc_f, mc_b, mrst_f, mrsb, ps[6],
                          QQh[64:65, :], 16)
        mrow_dg = self.cnt["dg"]

        if h == 0:
            self.maybe_dump("vext0", VEXT)
            self.maybe_dump("qq0m", QQh[:, :])
            self.maybe_dump("mcf0", mc_f)
        if h == 1:
            self.maybe_dump("vext1", VEXT)
            self.maybe_dump("qq1m", QQh[:, :])
            self.maybe_dump("kk1m", KKh[:, :])
        av_started = [False] * 4
        for kt in range(NTT):
            nq = (kt + 1) * 128
            nchk = (nq + 511) // 512
            for ci in range(nchk):
                w = min(512, nq - ci * 512)
                slot = (kt * 4 + ci) % 2

                @block.tensor
                def _(t):
                    self.wait(t, "t", "dg", mrow_dg)
                    self.wait(t, "t", "v", vext_done)
                    self.wait(t, "t", "s")
                    i = nc.tensor.matmul(
                        out=ps[ci % 2][:, :w],
                        lhsT=KKh[0:65, kt * 128:(kt + 1) * 128],
                        rhs=QQh[0:65, ci * 512:ci * 512 + w],
                        start=True, stop=True)
                    self.inc(i, "t")
                tq = self.cnt["t"]

                @block.scalar
                def _(sc):
                    self.wait(sc, "s", "t", tq)
                    i = nc.scalar.activation(out=ATT[slot][:, :w],
                                             in_=ps[ci % 2][:, :w],
                                             func=AF.Exp, bias=0.0,
                                             scale=0.03125)
                    self.inc(i, "s")
                sq = self.cnt["s"]

                lo, hi_ = ci * 512, ci * 512 + w
                dlo = kt * 128
                if lo <= dlo < hi_:
                    @block.vector
                    def _(ve):
                        self.wait(ve, "v", "s", sq)
                        i = ve.tensor_tensor(
                            out=ATT[slot][:, dlo - lo:dlo - lo + 128],
                            in0=ATT[slot][:, dlo - lo:dlo - lo + 128],
                            in1=self.mskB[:], op=ALU.mult)
                        self.inc(i, "v")
                    gate = ("v", self.cnt["v"])
                else:
                    gate = ("s", sq)

                if h == 0 and kt == 0 and ci == 0:
                    self.maybe_dump("att_k0", ATT[slot][:, :128])
                if h == 1 and kt == 0 and ci == 0:
                    self.maybe_dump("att1_k0", ATT[slot][:, :128])
                if h == 1 and kt == 2 and ci == 0:
                    self.maybe_dump("att1_k2", ATT[slot][:, :384])
                if h == 0 and kt == 3 and ci == 0:
                    self.maybe_dump("att_k3", ATT[slot][:, :512])
                @block.tensor
                def _(t):
                    self.wait(t, "t", gate[0], gate[1])
                    for qb in range(lo // 128, (ci * 512 + w) // 128):
                        bank = 2 + qb // 4
                        col = (qb % 4) * 66
                        first = (kt == qb)
                        i = nc.tensor.matmul(
                            out=ps[bank][:, col:col + 66],
                            lhsT=ATT[slot][:, qb * 128 - lo: qb * 128 - lo + 128],
                            rhs=VEXT.rearrange("p (t u) -> p t u", u=66)[:, kt, :],
                            start=(first and qb % 4 == 0),
                            stop=(kt == NTT - 1 and qb % 4 == 3),
                            skip_group_check=True)
                    self.inc(i, "t")
        av_done_t = self.cnt["t"]

        for g4 in range(4):
            bank = 2 + g4

            @block.vector
            def _(ve):
                self.wait(ve, "v", "t", av_done_t)
                if g4 > 0:
                    self.wait(ve, "v", "s")
                for qq in range(4):
                    i = ve.reciprocal(
                        out=rl_f[:, qq:qq + 1],
                        in_=ps[bank][:, qq * 66 + 64:qq * 66 + 65])
                self.inc(i, "v")
            vq = self.cnt["v"]
            if h == 1 and g4 == 0 and "avps1" in self.dump_names:
                self.maybe_dump("vext1b", VEXT)
                self.maybe_dump("wvulate", WVU)
            if False and h == 1:
                @block.scalar
                def _(sc):
                    self.self_wait(sc, "s")
                    self.wait(sc, "s", "t", av_done_t)
                    i = nc.scalar.copy(out=self.STG[:, :264], in_=ps[bank][:, :264])
                    self.inc(i, "s")
                self.maybe_dump("avps1", self.STG[:, :264])
            if h == 0 and g4 == 0 and "avps0" in self.dump_names:
                @block.scalar
                def _(sc):
                    self.self_wait(sc, "s")
                    self.wait(sc, "s", "t", av_done_t)
                    i = nc.scalar.copy(out=self.STG[:, :264], in_=ps[bank][:, :264])
                    self.inc(i, "s")
                self.maybe_dump("rl0", rl_f)
                self.maybe_dump("avps0", self.STG[:, :264])

            @block.scalar
            def _(sc):
                self.wait(sc, "s", "v", vq)
                for qq in range(4):
                    qb = g4 * 4 + qq
                    i = nc.scalar.activation(
                        out=AOP[:, qb * 128 + hi * 64: qb * 128 + hi * 64 + 64],
                        in_=ps[bank][:, qq * 66:qq * 66 + 64],
                        func=AF.Copy, bias=0.0, scale=rl_f[:, qq:qq + 1])
                    self.inc(i, "s")


    def emit_postln(self, b, spill):
        nc, block = self.nc, self.block
        X, ps = self.X, self.ps
        v = self.v
        XB2 = [v(0 + c * 2048, (128, 2048)) for c in range(NCH)]
        XSQ = [v(16384 + c * 2048, (128, 2048)) for c in range(NCH)]
        mean_r = v(36992, (1, 2048), F32)
        rstd_r = v(41088, (1, 2048), F32)
        m2_r = v(45184, (1, 2048), F32)
        hi_b = v(49280, (1, 2048))
        lo_f = v(51328, (1, 2048), F32)
        lo_b = v(55424, (1, 2048))
        tmp = v(57472, (128, 512), F32)

        @block.vector
        def _(ve):
            self.self_wait(ve, "v")
            self.wait(ve, "v", "t")
            self.wait(ve, "v", "s")
            for c in range(NCH):
                i = ve.tensor_copy(out=XB2[c],
                                   in_=X[:, c * CST + 1:c * CST + 1 + 2048])
            self.inc(i, "v")
        cast_done = self.cnt["v"]

        @block.scalar
        def _(sc):
            self.wait(sc, "s", "v", cast_done)
            for c in range(NCH):
                i = nc.scalar.activation(out=XSQ[c], in_=XB2[c],
                                         func=AF.Square, bias=0.0, scale=1.0)
            self.inc(i, "s")
        sq_done = self.cnt["s"]

        for tc in range(4):
            @block.tensor
            def _(t):
                self.wait(t, "t", "v", cast_done)
                self.wait(t, "t", "s", sq_done)
                self.wait(t, "t", "v")
                self.wait(t, "t", "s")
                for c in range(NCH):
                    nc.tensor.matmul(out=ps[5][0:1, :512], lhsT=self.onesc[:],
                                     rhs=XB2[c][:, tc * 512:(tc + 1) * 512],
                                     start=(c == 0), stop=(c == NCH - 1))
                for c in range(NCH):
                    i = nc.tensor.matmul(out=ps[6][0:1, :512], lhsT=self.onesc[:],
                                         rhs=XSQ[c][:, tc * 512:(tc + 1) * 512],
                                         start=(c == 0), stop=(c == NCH - 1))
                self.inc(i, "t")
            tq = self.cnt["t"]
            sl = slice(tc * 512, (tc + 1) * 512)

            @block.scalar
            def _(sc):
                self.wait(sc, "s", "t", tq)
                i = nc.scalar.activation(out=mean_r[:, sl], in_=ps[5][0:1, :512],
                                         func=AF.Copy, bias=0.0, scale=1.0 / D)
                self.inc(i, "s")
                self.self_wait(sc, "s")
                i = nc.scalar.activation(out=m2_r[:, sl], in_=mean_r[:, sl],
                                         func=AF.Square, bias=0.0, scale=1.0)
                self.inc(i, "s")
            sq2 = self.cnt["s"]

            @block.vector
            def _(ve):
                self.wait(ve, "v", "s", sq2)
                i = ve.scalar_tensor_tensor(out=rstd_r[:, sl], in0=ps[6][0:1, :512],
                                            scalar=1.0 / D, in1=m2_r[:, sl],
                                            op0=ALU.mult, op1=ALU.subtract)
                self.inc(i, "v")
            vq = self.cnt["v"]

            @block.vector
            def _(ve):
                self.self_wait(ve, "v")
                i = ve.tensor_single_scalar(out=rstd_r[:, sl], in_=rstd_r[:, sl],
                                            scalar=1e-5, op=ALU.add)
                self.inc(i, "v")
            vq2 = self.cnt["v"]

            @block.scalar
            def _(sc):
                self.wait(sc, "s", "v", vq2)
                i = nc.scalar.activation(out=rstd_r[:, sl], in_=rstd_r[:, sl],
                                         func=AF.Sqrt, bias=0.0, scale=1.0)
                self.inc(i, "s")
            sq3 = self.cnt["s"]

            @block.vector
            def _(ve):
                self.wait(ve, "v", "s", sq3)
                i = ve.reciprocal(out=rstd_r[:, sl], in_=rstd_r[:, sl])
                self.inc(i, "v")

        for tc in range(4):
            sl = slice(tc * 512, (tc + 1) * 512)
            self.bcast_rows(mean_r[:, sl], hi_b[:, sl], lo_f[:, sl],
                            lo_b[:, sl], ps[5][:, :512], 512)
            self.bcast_rows(rstd_r[:, sl], hi_b[:, sl], lo_f[:, sl],
                            lo_b[:, sl], ps[6][:, :512], 512)
            bc_done = self.cnt["t"]

            @block.vector
            def _(ve):
                self.wait(ve, "v", "t", bc_done)
                for c in range(NCH):
                    xa = X[:, c * CST + 1 + tc * 512:c * CST + 1 + (tc + 1) * 512]
                    i = ve.tensor_tensor(out=tmp, in0=xa, in1=ps[5][:, :512],
                                         op=ALU.subtract)
                    self.inc(i, "v")
                    self.self_wait(ve, "v")
                    i = ve.tensor_tensor(out=xa, in0=tmp, in1=ps[6][:, :512],
                                         op=ALU.mult)
                    self.inc(i, "v")
        apply_done = self.cnt["v"]

        if spill:
            @block.sync
            def _(sync):
                self.wait(sync, "sy", "v", apply_done)
                for c in range(NCH):
                    i = sync.dma_start(
                        out=self.YSP[b, c * 128:(c + 1) * 128, :],
                        in_=X[:, c * CST + 1:c * CST + 1 + 2048])
                    self.inc(i, "dw", 16)
            self.spill_gate = self.cnt["dw"]

    def emit_final(self, b):
        nc, block = self.nc, self.block
        X = self.X
        v = self.v
        YR = [v(c * 4096, (128, 2048), F32) for c in range(NCH)]

        for c in range(NCH):
            @block.sync
            def _(sync):
                self.wait(sync, "sy", "v")
                self.wait(sync, "sy", "t")
                i = sync.dma_start(out=YR[c],
                                   in_=self.YSP[b, c * 128:(c + 1) * 128, :])
                self.inc(i, "dw", 16)
            dmac = self.cnt["dw"]

            @block.vector
            def _(ve):
                self.wait(ve, "v", "dw", dmac)
                xa = X[:, c * CST + 1:c * CST + 1 + 2048]
                i = ve.tensor_tensor(out=xa, in0=xa, in1=YR[c], op=ALU.add)
                self.inc(i, "v")

        self.emit_postln(b, spill=False)


    def emit_inner_attn(self, b, l):
        nc, block = self.nc, self.block
        X, W, sm, ps, psb = self.X, self.W, self.sm, self.ps, self.psb
        v = self.v
        XS = [v(0 + c * 2048, (128, 2048)) for c in range(NCH)]
        AOP = v(16384, (128, 2048))
        QTK = {"q": v(18432, (128, 4096)), "k": v(22528, (128, 4096))}
        AOT = [v(26624 + c * 2048, (128, 2048)) for c in range(NCH)]
        VX = v(43008, (128, 4160)).rearrange("p (t h u) -> p t h u", h=4, u=65)
        WSL = {"q": v(47168, (128, 2048)), "k": v(49216, (128, 2048)),
               "v": v(51264, (128, 2048))}
        STAGE = [v(53312, (128, 1024), F32), v(55360, (128, 1024), F32)]
        QQ = v(53312, (65, 2048))
        KK = v(55360, (65, 2048))
        RING = [v(57408 + i * 384, (128, 384)) for i in range(3)]
        mc_f = v(58560, (128, 16), F32)
        mrst_f = v(58592, (16, 128), F32)
        rnq_all = v(58592, (128, 64), F32)[:, :64]
        rnk_all = v(58720, (128, 64), F32)[:, :64]
        mc_b = v(58848, (128, 16))
        mrsb = v(58864, (16, 128))
        rl_f = v(58992, (128, 4), F32)
        qs_bc = v(59008, (128, 64), F32)
        ks_bc = v(59136, (128, 64), F32)
        rows_b = v(59264, (1, 128))
        QF = self.STG[:, 0:256]
        KF = self.STG[:, 256:512]
        UF = self.STG[:, 512:768]
        WFr = self.STG[:, 768:1024]
        SQr = self.STG[:, 1024:1280]
        rcq, rsq = self.rtab["rcq"], self.rtab["rsq"]
        rck, rsk = self.rtab["rck"], self.rtab["rsk"]

        @block.vector
        def _(ve):
            self.self_wait(ve, "v")
            self.wait(ve, "v", "s")
            self.wait(ve, "v", "t")
            if getattr(self, "spill_gate", 0):
                self.wait(ve, "v", "dw", self.spill_gate)
            for c in range(NCH):
                off = 1 if c < 4 else 0
                i = ve.tensor_copy(out=XS[c],
                                   in_=X[:, c * CST + off:c * CST + off + 2048])
            self.inc(i, "v")
        xs_done = self.cnt["v"]

        for bcv, key in ((qs_bc, "qs_r"), (ks_bc, "ks_r")):
            @block.vector
            def _(ve):
                self.wait(ve, "v", "t")
                self.wait(ve, "v", "s")
                i = ve.tensor_copy(out=rows_b[:, :64],
                                   in_=sm[key][0:1, l * 64:(l + 1) * 64])
                self.inc(i, "v")
            vq = self.cnt["v"]

            @block.tensor
            def _(t):
                self.wait(t, "t", "v", vq)
                self.wait(t, "t", "s")
                i = nc.tensor.matmul(out=ps[6][:, :64], lhsT=self.onesr[:],
                                     rhs=rows_b[:, :64], start=True, stop=True)
                self.inc(i, "t")
            tq = self.cnt["t"]

            @block.scalar
            def _(sc):
                self.wait(sc, "s", "t", tq)
                i = nc.scalar.copy(out=bcv, in_=ps[6][:, :64])
                self.inc(i, "s")
        bc_done_s = self.cnt["s"]

        WSLA = v(47168, (128, 6144))
        for g in range(4):
            gate = (("t", self.cnt["t"]), ("dg", self.cnt["dg"]))
            for c in range(NCH):
                dram = (W["wqkv"][l, c * 128:(c + 1) * 128, :]
                        .rearrange("p (k f) -> p k f", f=256)[:, [0, 4, 8], :]
                        if False else
                        W["wqkv"][l, c * 128:(c + 1) * 128, g * 256:]
                        .rearrange("p (k f) -> p k f", f=256)[:, 0:9:4, :])
                self.load_cast(
                    dram,
                    WSLA[:, c * 768:(c + 1) * 768]
                    .rearrange("p (k f) -> p k f", f=256),
                    STAGE[c % 2][:, :768].rearrange("p (k f) -> p k f", f=256),
                    pre_gate=gate)
            w_ready = self.cnt["v"]

            @block.vector
            def _(ve):
                self.wait(ve, "v", "t")
                i = ve.memset(VX[:, :, :, 64:65], 1.0)
                self.inc(i, "v")

            for tt in range(NTT):
                @block.tensor
                def _(t):
                    self.wait(t, "t", "v", max(w_ready, xs_done))
                    self.wait(t, "t", "s")
                    for bi in range(3):
                        for c in range(NCH):
                            mm = nc.tensor.matmul(
                                out=ps[bi][:, :256],
                                lhsT=XS[c][:, tt * 128:(tt + 1) * 128],
                                rhs=WSLA[:, c * 768 + bi * 256:
                                         c * 768 + bi * 256 + 256],
                                start=(c == 0), stop=(c == NCH - 1))
                        self.inc(mm, "t")
                tq = self.cnt["t"]

                @block.scalar
                def _(sc):
                    self.wait(sc, "s", "t", tq)
                    self.wait(sc, "s", "v")
                    nc.scalar.copy(out=QF, in_=ps[0][:, :256])
                    nc.scalar.copy(out=KF, in_=ps[1][:, :256])
                    i = nc.scalar.copy(
                        out=VX[:, tt, :, 0:64],
                        in_=ps[2][:, :256])
                    self.inc(i, "s")
                    self.self_wait(sc, "s")
                    i = nc.scalar.activation(out=SQr, in_=QF, func=AF.Square,
                                             bias=0.0, scale=1.0)
                    self.inc(i, "s")
                ev_done = self.cnt["s"]

                @block.vector
                def _(ve):
                    self.wait(ve, "v", "s", ev_done)
                    i = ve.tensor_reduce(
                        out=rnq_all[:, tt * 4:(tt + 1) * 4],
                        in_=SQr.rearrange("p (h u) -> p h u", u=64),
                        axis=AX.X, op=ALU.add)
                    self.inc(i, "v")
                vq = self.cnt["v"]

                @block.scalar
                def _(sc):
                    self.wait(sc, "s", "v", vq)
                    i = nc.scalar.activation(out=SQr, in_=KF, func=AF.Square,
                                             bias=0.0, scale=1.0)
                    self.inc(i, "s")
                sq2 = self.cnt["s"]

                @block.vector
                def _(ve):
                    self.wait(ve, "v", "s", sq2)
                    i = ve.tensor_reduce(
                        out=rnk_all[:, tt * 4:(tt + 1) * 4],
                        in_=SQr.rearrange("p (h u) -> p h u", u=64),
                        axis=AX.X, op=ALU.add)
                    self.inc(i, "v")

                for which, PF, bcv, ctab, stab in (
                        ("q", QF, qs_bc, rcq, rsq), ("k", KF, ks_bc, rck, rsk)):
                    @block.vector
                    def _(ve):
                        self.self_wait(ve, "v")
                        self.wait(ve, "v", "s")
                        i = ve.tensor_tensor(
                            out=PF.rearrange("p (h u) -> p h u", u=64),
                            in0=PF.rearrange("p (h u) -> p h u", u=64),
                            in1=bcv.unsqueeze(1).broadcast_to((128, 4, 64)),
                            op=ALU.mult)
                        self.inc(i, "v")
                        self.self_wait(ve, "v")
                        i = ve.tensor_tensor(
                            out=UF.rearrange("p (h u) -> p h u", u=64),
                            in0=PF.rearrange("p (h u) -> p h u", u=64),
                            in1=ctab[:, tt * 64:(tt + 1) * 64]
                            .unsqueeze(1).broadcast_to((128, 4, 64)),
                            op=ALU.mult)
                        self.inc(i, "v")
                        P4 = PF.rearrange("p (h x u) -> p h x u", x=2, u=32)
                        W4 = WFr.rearrange("p (h x u) -> p h x u", x=2, u=32)
                        S4 = stab[:, tt * 64:(tt + 1) * 64].rearrange(
                            "p (x u) -> p x u", x=2).unsqueeze(1)
                        i = ve.tensor_tensor(
                            out=W4[:, :, 0, :], in0=P4[:, :, 1, :],
                            in1=S4[:, :, 0, :].broadcast_to((128, 4, 32)),
                            op=ALU.mult)
                        i = ve.tensor_tensor(
                            out=W4[:, :, 1, :], in0=P4[:, :, 0, :],
                            in1=S4[:, :, 1, :].broadcast_to((128, 4, 32)),
                            op=ALU.mult)
                        self.inc(i, "v")
                        self.self_wait(ve, "v")
                        i = ve.tensor_tensor(
                            out=QTK[which][:, tt * 256:(tt + 1) * 256],
                            in0=UF, in1=WFr, op=ALU.add)
                        self.inc(i, "v")

            for which, rna in (("q", rnq_all), ("k", rnk_all)):
                @block.scalar
                def _(sc):
                    self.wait(sc, "s", "v")
                    i = nc.scalar.activation(out=rna, in_=rna, func=AF.Sqrt,
                                             bias=0.0, scale=1.0)
                    self.inc(i, "s")
                sq3 = self.cnt["s"]

                @block.vector
                def _(ve):
                    self.wait(ve, "v", "s", sq3)
                    i = ve.tensor_single_scalar(out=rna, in_=rna,
                                                scalar=1e-12, op=ALU.max)
                    self.inc(i, "v")
                    self.self_wait(ve, "v")
                    i = ve.reciprocal(out=rna, in_=rna)
                    self.inc(i, "v")
                    self.self_wait(ve, "v")
                    for tt in range(NTT):
                        i = ve.tensor_tensor(
                            out=QTK[which][:, tt * 256:(tt + 1) * 256]
                            .rearrange("p (h u) -> p h u", u=64),
                            in0=QTK[which][:, tt * 256:(tt + 1) * 256]
                            .rearrange("p (h u) -> p h u", u=64),
                            in1=rna[:, tt * 4:(tt + 1) * 4]
                            .unsqueeze(2).broadcast_to((128, 4, 64)),
                            op=ALU.mult)
                    self.inc(i, "v")
            rot_done_v = self.cnt["v"]

            if g == 0 and b == 0 and l == 0:
                self.maybe_dump("qtk_q", QTK["q"][:, :])
                self.maybe_dump("qtk_k", QTK["k"][:, :])
                self.maybe_dump("vx0", VX.rearrange("p t h u -> p (t h u)"))
            for hh in range(4):
                self.emit_inner_head(b, l, g, hh, rot_done_v, QTK, QQ, KK, VX,
                                     RING, AOP, mc_f, mc_b, mrst_f, mrsb, rl_f)
                if hh % 2 == 1:
                    self.emit_inner_pair_fin(b, l, g, hh // 2, AOP, AOT)

        if l == 0 and b == 0:
            for c in range(NCH):
                self.maybe_dump(f"iaot{c}", AOT[c])
        self.emit_inner_wo(b, l, AOT, WSL, STAGE)

    def emit_inner_head(self, b, l, g, hh, rot_done_v, QTK, QQ, KK, VX, RING,
                        AOP, mc_f, mc_b, mrst_f, mrsb, rl_f):
        nc, block, ps, psb = self.nc, self.block, self.ps, self.psb

        for which, DD in (("q", QQ), ("k", KK)):
            for grp in range(2):
                @block.tensor
                def _(t):
                    self.wait(t, "t", "v", rot_done_v)
                    self.wait(t, "t", "s")
                    for j in range(8):
                        tt = grp * 8 + j
                        i = nc.tensor.transpose(
                            out=psb[0:64, j * 128:(j + 1) * 128],
                            in_=QTK[which][:, tt * 256 + hh * 64:
                                           tt * 256 + hh * 64 + 64],
                            identity=self.idb[:])
                    self.inc(i, "t")
                tq = self.cnt["t"]

                @block.scalar
                def _(sc):
                    self.wait(sc, "s", "t", tq)
                    i = nc.scalar.copy(
                        out=DD[0:64, grp * 1024:(grp + 1) * 1024],
                        in_=psb[0:64, :1024])
                    self.inc(i, "s")
        tp_done = self.cnt["s"]

        @block.vector
        def _(ve):
            self.wait(ve, "v", "t")
            i = ve.memset(KK[64:65, :], 1.0)
            self.inc(i, "v")
        ones_done = self.cnt["v"]

        red_gate = {}
        for qb in range(NTT):
            k0 = max(0, qb - 2) * 128
            kw = (qb + 1) * 128 - k0

            @block.tensor
            def _(t):
                self.wait(t, "t", "s", tp_done)
                if qb == 0:
                    self.wait(t, "t", "v")
                if qb - 4 in red_gate:
                    self.wait(t, "t", "v", red_gate[qb - 4])
                i = nc.tensor.matmul(
                    out=ps[qb % 4][:, :kw],
                    lhsT=QQ[0:64, qb * 128:(qb + 1) * 128],
                    rhs=KK[0:64, k0:k0 + kw],
                    start=True, stop=True)
                self.inc(i, "t")
            tq = self.cnt["t"]

            @block.vector
            def _(ve):
                self.wait(ve, "v", "t", tq)
                i = ve.tensor_reduce(out=mc_f[:, qb:qb + 1],
                                     in_=ps[qb % 4][:, :kw], axis=AX.X,
                                     op=ALU.max, negate=True)
                self.inc(i, "v")
            red_gate[qb] = self.cnt["v"]

        self.m_row_to_dst(mc_f, mc_b, mrst_f, mrsb, ps[6], QQ[64:65, :], 16)
        mrow_dg = self.cnt["dg"]

        if g == 0 and hh == 0 and l == 0 and b == 0:
            self.maybe_dump("iqq0", QQ[:, :])
            self.maybe_dump("ikk0", KK[:, :])
            self.maybe_dump("imc0", mc_f)
        stbank = [2, 3, 6]
        exp_gate = {}
        for kt in range(NTT):
            nq = min(3, NTT - kt) * 128
            slot = RING[kt % 3]

            @block.tensor
            def _(t):
                self.wait(t, "t", "dg", mrow_dg)
                self.wait(t, "t", "v", ones_done)
                if kt == 0:
                    self.wait(t, "t", "s")
                if kt - 3 in exp_gate:
                    self.wait(t, "t", "s", exp_gate[kt - 3])
                i = nc.tensor.matmul(
                    out=ps[stbank[kt % 3]][:, :nq],
                    lhsT=KK[0:65, kt * 128:(kt + 1) * 128],
                    rhs=QQ[0:65, kt * 128:kt * 128 + nq],
                    start=True, stop=True)
                self.inc(i, "t")
            tq = self.cnt["t"]

            @block.scalar
            def _(sc):
                self.wait(sc, "s", "t", tq)
                i = nc.scalar.activation(out=slot[:, :nq],
                                         in_=ps[stbank[kt % 3]][:, :nq],
                                         func=AF.Exp, bias=0.0, scale=8.0)
                self.inc(i, "s")
            sq = self.cnt["s"]
            exp_gate[kt] = sq

            @block.vector
            def _(ve):
                self.wait(ve, "v", "s", sq)
                i = ve.tensor_tensor(out=slot[:, 0:128], in0=slot[:, 0:128],
                                     in1=self.mskA[:], op=ALU.mult)
                self.inc(i, "v")
                if nq == 384:
                    self.self_wait(ve, "v")
                    i = ve.tensor_tensor(out=slot[:, 256:384],
                                         in0=slot[:, 256:384],
                                         in1=self.mskB[:], op=ALU.mult)
                    self.inc(i, "v")
            mask_v = self.cnt["v"]

            qb = kt
            bank = 4 + (qb // 4) % 2
            col = (qb % 4) * 65

            @block.tensor
            def _(t):
                self.wait(t, "t", "v", mask_v)
                if qb % 4 == 0 and qb >= 4:
                    self.wait(t, "t", "s")
                first = True
                for kp in range(max(0, qb - 2), qb + 1):
                    rel = (qb - kp) * 128
                    i = nc.tensor.matmul(
                        out=ps[bank][:, col:col + 65],
                        lhsT=RING[kp % 3][:, rel:rel + 128],
                        rhs=VX[:, kp, hh, :],
                        start=(first and qb % 4 == 0),
                        stop=(kp == qb and qb % 4 == 3),
                        skip_group_check=True)
                    first = False
                self.inc(i, "t")

            if qb % 4 == 3:
                av_t = self.cnt["t"]

                @block.vector
                def _(ve):
                    self.wait(ve, "v", "t", av_t)
                    if qb > 3:
                        self.wait(ve, "v", "s")
                    for qq in range(4):
                        i = ve.reciprocal(
                            out=rl_f[:, qq:qq + 1],
                            in_=ps[bank][:, qq * 65 + 64:qq * 65 + 65])
                    self.inc(i, "v")
                vq = self.cnt["v"]

                @block.scalar
                def _(sc):
                    self.wait(sc, "s", "v", vq)
                    for qq in range(4):
                        qbb = (qb // 4) * 4 + qq
                        i = nc.scalar.activation(
                            out=AOP[:, qbb * 128 + (hh % 2) * 64:
                                    qbb * 128 + (hh % 2) * 64 + 64],
                            in_=ps[bank][:, qq * 65:qq * 65 + 64],
                            func=AF.Copy, bias=0.0, scale=rl_f[:, qq:qq + 1])
                        self.inc(i, "s")

    def emit_inner_pair_fin(self, b, l, g, pp, AOP, AOT):
        nc, block, ps, psb = self.nc, self.block, self.ps, self.psb
        P = 2 * g + pp
        if P == 0 and l == 0 and b == 0:
            self.maybe_dump("iaop0", AOP)
        for grp in range(2):
            @block.tensor
            def _(t):
                self.wait(t, "t", "s")
                for j in range(8):
                    tt = grp * 8 + j
                    i = nc.tensor.transpose(
                        out=psb[:, j * 128:(j + 1) * 128],
                        in_=AOP[:, tt * 128:(tt + 1) * 128],
                        identity=self.idb[:])
                self.inc(i, "t")
            tq = self.cnt["t"]

            @block.scalar
            def _(sc):
                self.wait(sc, "s", "t", tq)
                i = nc.scalar.copy(out=AOT[P][:, grp * 1024:(grp + 1) * 1024],
                                   in_=psb[:, :1024])
                self.inc(i, "s")

    def emit_inner_wo(self, b, l, AOT, WSL, STAGE):
        nc, block, ps = self.nc, self.block, self.ps
        W = self.W
        v = self.v
        gate = (("t", self.cnt["t"]), ("dg", self.cnt["dg"]))
        WOB = v(0, (128, 8192))
        for ic in range(NCH):
            self.load_cast(
                W["wo"][l, ic * 128:(ic + 1) * 128, :],
                WOB[:, ic * 1024:(ic + 1) * 1024],
                STAGE[ic % 2], pre_gate=gate)
        w_ready = self.cnt["v"]

        for fc in range(NCH):
            for tc in range(4):
                @block.tensor
                def _(t):
                    self.wait(t, "t", "v", w_ready)
                    self.wait(t, "t", "s")
                    self.wait(t, "t", "v")
                    for ic in range(NCH):
                        i = nc.tensor.matmul(
                            out=ps[tc % 2][:, :512],
                            lhsT=WOB[:, ic * 1024 + fc * 128:
                                     ic * 1024 + fc * 128 + 128],
                            rhs=AOT[ic][:, tc * 512:(tc + 1) * 512],
                            start=(ic == 0), stop=(ic == NCH - 1))
                    self.inc(i, "t")
                tq = self.cnt["t"]

                @block.vector
                def _(ve):
                    self.wait(ve, "v", "t", tq)
                    xa = self.X[:, fc * CST + 1 + tc * 512:
                                fc * CST + 1 + (tc + 1) * 512]
                    i = ve.tensor_tensor(out=xa, in0=xa, in1=ps[tc % 2][:, :512],
                                         op=ALU.add)
                    self.inc(i, "v")

    def emit_inner_ffn(self, b, l):
        nc, block, ps = self.nc, self.block, self.ps
        X, W, sm = self.X, self.W, self.sm
        v = self.v
        W1B = [v(0 + c * 5460, (128, 5460)) for c in range(NCH)]
        H = v(43680, (128, 5632))
        XS2 = [v(43680 + c * 256, (128, 256)) for c in range(NCH)]
        XSQ2 = [v(45728 + c * 256, (128, 256)) for c in range(NCH)]
        XN = [v(49312 + c * 256, (128, 256)) for c in range(NCH)]
        W2B = [v(51360 + i * 1024, (128, 1024)) for i in range(2)]
        STAGE = v(53408, (128, 1024), F32)
        mean_r = v(55456, (1, 256), F32)
        m2_r = v(55968, (1, 256), F32)
        rstd_r = v(56480, (1, 256), F32)
        lo_f = v(56992, (1, 256), F32)
        hi_b = v(57504, (1, 256))
        lo_b = v(57760, (1, 256))
        g1 = v(58016, (128, 256))
        tmp = v(58272, (128, 256), F32)
        shc = v(58784, (128, 8), F32)

        gate0 = (("t", self.cnt["t"]), ("dg", self.cnt["dg"]))

        atiles = [(128 * j, min(128, DIH - 128 * j)) for j in range((DIH + 127) // 128)]
        W1ALL = self.v(0, (128, 8 * 5460))
        fslices = [(k * 1024, min(1024, DI - k * 1024)) for k in range(6)]
        for c in range(NCH):
            for f0, fw in fslices:
                self.load_cast(
                    W["w1"][l, c * 128:(c + 1) * 128, f0:f0 + fw],
                    W1ALL[:, c * 5460 + f0:c * 5460 + f0 + fw],
                    STAGE[:, :fw], pre_gate=gate0)
        w1_ready = self.cnt["v"]

        for tc in range(8):
            t0 = tc * 256
            @block.vector
            def _(ve):
                self.self_wait(ve, "v")
                self.wait(ve, "v", "s")
                self.wait(ve, "v", "t")
                for c in range(NCH):
                    off = 1 if c < 4 else 0
                    i = ve.tensor_copy(
                        out=XS2[c],
                        in_=X[:, c * CST + off + t0:c * CST + off + t0 + 256])
                self.inc(i, "v")
            cast_done = self.cnt["v"]

            @block.scalar
            def _(sc):
                self.wait(sc, "s", "v", cast_done)
                for c in range(NCH):
                    i = nc.scalar.activation(out=XSQ2[c], in_=XS2[c],
                                             func=AF.Square, bias=0.0, scale=1.0)
                self.inc(i, "s")
            sq_done = self.cnt["s"]

            @block.tensor
            def _(t):
                self.wait(t, "t", "s", sq_done)
                self.wait(t, "t", "v")
                self.wait(t, "t", "s")
                for c in range(NCH):
                    nc.tensor.matmul(out=ps[0][0:1, :256], lhsT=self.onesc[:],
                                     rhs=XS2[c], start=(c == 0),
                                     stop=(c == NCH - 1))
                for c in range(NCH):
                    i = nc.tensor.matmul(out=ps[1][0:1, :256],
                                         lhsT=self.onesc[:], rhs=XSQ2[c],
                                         start=(c == 0), stop=(c == NCH - 1))
                self.inc(i, "t")
            tq = self.cnt["t"]

            @block.scalar
            def _(sc):
                self.wait(sc, "s", "t", tq)
                i = nc.scalar.activation(out=mean_r, in_=ps[0][0:1, :256],
                                         func=AF.Copy, bias=0.0, scale=1.0 / D)
                self.inc(i, "s")
                self.self_wait(sc, "s")
                i = nc.scalar.activation(out=m2_r, in_=mean_r, func=AF.Square,
                                         bias=0.0, scale=1.0)
                self.inc(i, "s")
            sq2 = self.cnt["s"]

            @block.vector
            def _(ve):
                self.wait(ve, "v", "s", sq2)
                i = ve.scalar_tensor_tensor(out=rstd_r, in0=ps[1][0:1, :256],
                                            scalar=1.0 / D, in1=m2_r,
                                            op0=ALU.mult, op1=ALU.subtract)
                self.inc(i, "v")
                self.self_wait(ve, "v")
                i = ve.tensor_single_scalar(out=rstd_r, in_=rstd_r,
                                            scalar=1e-5, op=ALU.add)
                self.inc(i, "v")
            vq = self.cnt["v"]

            @block.scalar
            def _(sc):
                self.wait(sc, "s", "v", vq)
                i = nc.scalar.activation(out=rstd_r, in_=rstd_r, func=AF.Sqrt,
                                         bias=0.0, scale=1.0)
                self.inc(i, "s")
            sq3 = self.cnt["s"]

            @block.vector
            def _(ve):
                self.wait(ve, "v", "s", sq3)
                i = ve.reciprocal(out=rstd_r, in_=rstd_r)
                self.inc(i, "v")

            self.bcast_rows(mean_r, hi_b, lo_f, lo_b, ps[0][:, :256], 256)
            self.bcast_rows(rstd_r, hi_b, lo_f, lo_b, ps[1][:, :256], 256)
            bc_done = self.cnt["t"]

            for c in range(NCH):
                @block.vector
                def _(ve):
                    self.wait(ve, "v", "t", bc_done)
                    self.wait(ve, "v", "s")
                    off = 1 if c < 4 else 0
                    xa = X[:, c * CST + off + t0:c * CST + off + t0 + 256]
                    i = ve.tensor_tensor(out=tmp, in0=xa, in1=ps[0][:, :256],
                                         op=ALU.subtract)
                    self.inc(i, "v")
                    self.self_wait(ve, "v")
                    i = ve.tensor_tensor(out=tmp, in0=tmp, in1=ps[1][:, :256],
                                         op=ALU.mult)
                    self.inc(i, "v")
                vq = self.cnt["v"]

                @block.scalar
                def _(sc):
                    self.wait(sc, "s", "v", vq)
                    i = nc.scalar.activation(
                        out=XN[c], in_=tmp, func=AF.Identity,
                        bias=sm["b_cols"][:, l, c:c + 1],
                        scale=sm["g_cols"][:, l, c:c + 1])
                    self.inc(i, "s")
            xn_done = self.cnt["s"]
            if tc > 0:
                @block.vector
                def _(ve):
                    self.wait(ve, "v", "s", xn_done)
                    for fo in range(NCH):
                        xa = self.X[:, fo * CST + 1 + t0 - 1:fo * CST + 1 + t0]
                        i = ve.tensor_tensor(out=xa, in0=xa,
                                             in1=shc[:, fo:fo + 1], op=ALU.add)
                    self.inc(i, "v")
            if tc == 0 and l == 0 and b == 0:
                self.maybe_dump("fxn", self.v(49312, (128, 2048)))

            for j, (f0, wj) in enumerate(atiles):
                @block.tensor
                def _(t):
                    self.wait(t, "t", "v", w1_ready)
                    self.wait(t, "t", "s", xn_done)
                    self.wait(t, "t", "v")
                    self.wait(t, "t", "s")
                    W1ALL = self.v(0, (128, 8 * 5460))
                    WV = W1ALL.rearrange("p (c f) -> p c f", f=5460)
                    for c in range(NCH):
                        nc.tensor.matmul(
                            out=ps[2][:wj, :256],
                            lhsT=WV[:, c, DIH + f0:DIH + f0 + wj],
                            rhs=XN[c], start=(c == 0), stop=(c == NCH - 1))
                    for c in range(NCH):
                        i = nc.tensor.matmul(
                            out=ps[3][:wj, :256],
                            lhsT=WV[:, c, f0:f0 + wj],
                            rhs=XN[c], start=(c == 0), stop=(c == NCH - 1))
                    self.inc(i, "t")
                tq = self.cnt["t"]

                @block.scalar
                def _(sc):
                    self.wait(sc, "s", "t", tq)
                    self.wait(sc, "s", "v")
                    i = nc.scalar.activation(
                        out=g1[:wj, :], in_=ps[2][:wj, :256], func=AF.Gelu,
                        bias=sm["b1_cols"][:wj, l, 1, j:j + 1], scale=1.0)
                    self.inc(i, "s")
                sqg = self.cnt["s"]

                @block.vector
                def _(ve):
                    self.wait(ve, "v", "s", sqg)
                    i = ve.scalar_tensor_tensor(
                        out=H[:wj, j * 256:(j + 1) * 256],
                        in0=ps[3][:wj, :256],
                        scalar=sm["b1_cols"][:wj, l, 0, j:j + 1],
                        in1=g1[:wj, :], op0=ALU.add, op1=ALU.mult)
                    self.inc(i, "v")
            h_done = self.cnt["v"]
            if tc == 0 and l == 0 and b == 0:
                self.maybe_dump("fh", self.v(43680, (128, 5632)))

            for j, (f0, wj) in enumerate(atiles):
                self.load_cast(
                    W["w2"][l, f0:f0 + wj, :],
                    W2B[j % 2][:wj, :], STAGE[:wj, :],
                    pre_gate=gate0 if tc == 0 and j < 2 else ())
                w2r = self.cnt["v"]

                @block.tensor
                def _(t):
                    self.wait(t, "t", "v", max(w2r, h_done))
                    if j == 0:
                        self.wait(t, "t", "v")
                    for fo in range(NCH):
                        i = nc.tensor.matmul(
                            out=ps[2 + fo // 2][:, (fo % 2) * 256:
                                                (fo % 2) * 256 + 256],
                            lhsT=W2B[j % 2][:wj, fo * 128:(fo + 1) * 128],
                            rhs=H[:wj, j * 256:(j + 1) * 256],
                            start=(j == 0 and fo % 2 == 0),
                            stop=(j == NJ - 1 and fo % 2 == 1),
                            skip_group_check=True)
                    self.inc(i, "t")
            w2_done = self.cnt["t"]

            @block.vector
            def _(ve):
                self.wait(ve, "v", "t", w2_done)
                for fo in range(NCH):
                    xa = X[:, fo * CST + 1 + t0:fo * CST + 1 + t0 + 255]
                    i = ve.scalar_tensor_tensor(
                        out=xa, in0=ps[2 + fo // 2][:, (fo % 2) * 256:
                                                    (fo % 2) * 256 + 255],
                        scalar=sm["b2_cols"][:, l, fo:fo + 1],
                        in1=xa, op0=ALU.add, op1=ALU.add)
                    self.inc(i, "v")
                    i = ve.tensor_scalar(
                        out=shc[:, fo:fo + 1],
                        in0=ps[2 + fo // 2][:, (fo % 2) * 256 + 255:
                                            (fo % 2) * 256 + 256],
                        scalar1=sm["b2_cols"][:, l, fo:fo + 1], scalar2=None,
                        op0=ALU.add)
                    self.inc(i, "v")
            if tc == 7:
                @block.vector
                def _(ve):
                    self.self_wait(ve, "v")
                    for fo in range(NCH):
                        xa = self.X[:, fo * CST + 1 + t0 + 255:
                                    fo * CST + 1 + t0 + 256]
                        i = ve.tensor_tensor(out=xa, in0=xa,
                                             in1=shc[:, fo:fo + 1], op=ALU.add)
                    self.inc(i, "v")

    def emit_tail(self):
        block = self.block
        fin_dw = self.cnt["dw"]
        fin_dg = self.cnt["dg"]

        @block.sync
        def _(sync):
            sync.wait_ge(self.sems["dw"], fin_dw)
            if fin_dg:
                sync.wait_ge(self.sems["dg"], fin_dg)


def build(n_batch=None, n_layers=None, phases=None):
    kb = KB(n_batch or N_BATCH, n_layers or N_LAYERS, phases or PHASES)
    nc = kb.build()
    return nc, kb


def kernel(**inputs):
    n_cores = 8
    x = _f32(inputs["L0_attn_x"])
    consts = host_constants()
    lay = host_layouts(inputs)
    nc, kb = build()
    in_maps = []
    for core in range(n_cores):
        m = {"x_in": x[2 * core:2 * core + 2]}
        m.update(consts)
        m.update(lay)
        in_maps.append(m)
    res = run_bass_kernel_spmd(nc, in_maps, core_ids=list(range(n_cores)))
    return np.concatenate([np.asarray(r["out"], dtype=np.float32)
                           for r in res.results], axis=0)
